# version 08
# baseline (speedup 1.0000x reference)
import numpy as np
import concourse.bass as bass
import concourse.mybir as mybir
from concourse.bass_utils import run_bass_kernel_spmd
from concourse.bass import IndirectOffsetOnAxis
from contextlib import ExitStack

F32 = mybir.dt.float32
BF16 = mybir.dt.bfloat16
I32 = mybir.dt.int32
AF = mybir.ActivationFunctionType
OP = mybir.AluOpType
AX = mybir.AxisListType

P = 128
B = 2
N = 2048
TOK = B * N
D = 1024
H = 8
DH = 64
NQ = 512
NKV = 1024
NITERS = 50
BISECT = 34
EPSK = 2.0 ** -12
LNEPS = 1e-5
TC = 128
TCP1 = TC + 1
NCHUNK = N // TC
NTILE = TOK // P

DEBUG_PHASES = 99
P4_NCH = 99
P4_PARTS = 9
P4_NGI = 99
P6_NH = 99
P6_NB = 99
TAPS = []


class Sched:
    ENGINES = ("sy", "gp", "ve", "ac", "te")
    HAZ = {"ve": 2, "ac": 1, "gp": 0, "te": 0, "sy": None}

    def __init__(self):
        self.ops = {e: [] for e in self.ENGINES}
        self.lw = {}
        self.rd = {}
        self.deps = []
        self.spacer_fn = None

    def add(self, eng, emit, reads=(), writes=(), dma=False, tag="", haz_ok=()):
        need = self.HAZ[eng]
        if need is not None and not dma and self.spacer_fn is not None:
            idx0 = len(self.ops[eng])
            want = 0
            for k in list(reads) + list(writes):
                if k in haz_ok:
                    continue
                cands = []
                lwv = self.lw.get(k)
                if lwv is not None and lwv[0] == eng and not self.ops[eng][lwv[1]][1]:
                    cands.append(lwv[1])
                if k in writes:
                    for re_, ri in self.rd.get(k, ()):
                        if re_ == eng and not self.ops[eng][ri][1]:
                            cands.append(ri)
                for pi in cands:
                    short = need - (idx0 - pi - 1)
                    if short > want:
                        want = short
            for _ in range(want):
                self.ops[eng].append((self.spacer_fn(eng), False, "autosp"))
        idx = len(self.ops[eng])
        me = (eng, idx)
        for k in reads:
            lwv = self.lw.get(k)
            if lwv is not None:
                pe, pi = lwv
                if pe != eng:
                    self.deps.append((me, lwv))
                else:
                    pdma = self.ops[pe][pi][1]
                    if need is None or pdma or dma:
                        self.deps.append((me, lwv))
                    elif idx - pi - 1 < need and k not in haz_ok:
                        raise RuntimeError(
                            f"RAW {eng} key={k} dist={idx - pi - 1} need={need} "
                            f"tag={tag} prod={self.ops[eng][pi][2]}")
            self.rd.setdefault(k, []).append(me)
        for k in writes:
            for r in self.rd.get(k, ()):
                if r == me:
                    continue
                re_, ri = r
                if re_ != eng:
                    self.deps.append((me, r))
                else:
                    rdma = self.ops[re_][ri][1]
                    if need is None or rdma or dma:
                        self.deps.append((me, r))
                    elif idx - ri - 1 < need and k not in haz_ok:
                        raise RuntimeError(
                            f"WAR {eng} key={k} dist={idx - ri - 1} need={need} "
                            f"tag={tag} reader={self.ops[eng][ri][2]}")
            lwv = self.lw.get(k)
            if lwv is not None:
                pe, pi = lwv
                if pe != eng:
                    self.deps.append((me, lwv))
                else:
                    pdma = self.ops[pe][pi][1]
                    if need is None or pdma or dma:
                        self.deps.append((me, lwv))
                    elif idx - pi - 1 < need and k not in haz_ok:
                        raise RuntimeError(
                            f"WAW {eng} key={k} dist={idx - pi - 1} tag={tag}")
            self.rd[k] = []
            self.lw[k] = me
        self.ops[eng].append((emit, dma, tag))
        return me

    def barrier(self):
        pts = {e: len(self.ops[e]) for e in self.ENGINES}
        for e in self.ENGINES:
            me = (e, len(self.ops[e]))
            for pe in self.ENGINES:
                if pe != e and pts[pe] > 0:
                    self.deps.append((me, (pe, pts[pe] - 1)))
            self.ops[e].append((None, False, "barrier"))
        self.lw = {}
        self.rd = {}

    def schedule(self, nc, stack):
        marked = {e: set() for e in self.ENGINES}
        for (_ce, _ci), (pe, pi) in self.deps:
            marked[pe].add(pi)
        self.cum = {}
        self.incamt = {}
        for e in self.ENGINES:
            cum = 0
            cums = []
            for i, op in enumerate(self.ops[e]):
                amt = (16 if (op[1] or e == "sy") else 1) \
                    if (i in marked[e] or op[1]) else 0
                cum += amt
                cums.append(cum)
                self.incamt[(e, i)] = amt
            self.cum[e] = cums
        self.waits = {}
        for (ce, ci), (pe, pi) in self.deps:
            d = self.waits.setdefault((ce, ci), {})
            v = self.cum[pe][pi]
            if d.get(pe, -1) < v:
                d[pe] = v
        self.sems = {e: stack.enter_context(nc.semaphore(f"sem_{e}"))
                     for e in self.ENGINES}

    def emit(self, eng, eo):
        hw = {e: 0 for e in self.ENGINES}
        for i, (fn, dma, tag) in enumerate(self.ops[eng]):
            w = self.waits.get((eng, i))
            if w:
                for pe, val in sorted(w.items()):
                    if val > hw[pe]:
                        eo.wait_ge(self.sems[pe], val)
                        hw[pe] = val
            amt = self.incamt[(eng, i)]
            if fn is None:
                if amt:
                    eo.nop().then_inc(self.sems[eng], amt)
                continue
            try:
                inst = fn(eo)
                if amt:
                    if inst is None:
                        eo.nop().then_inc(self.sems[eng], amt)
                    else:
                        inst.then_inc(self.sems[eng], amt)
            except Exception:
                print(f"EMIT-FAIL eng={eng} i={i} tag={tag} amt={amt}")
                raise


class Env:
    pass


def TT(out, a, bb, op):
    return lambda e: e.tensor_tensor(out=out, in0=a, in1=bb, op=op)


def TS(out, a, s1, op0, s2=None, op1=None):
    if op1 is None:
        return lambda e: e.tensor_scalar(out=out, in0=a, scalar1=s1, scalar2=None, op0=op0)
    return lambda e: e.tensor_scalar(out=out, in0=a, scalar1=s1, scalar2=s2, op0=op0, op1=op1)


def CP(out, a):
    return lambda e: e.tensor_copy(out=out, in_=a)


def MS(ap, v):
    return lambda e: e.memset(ap, v)


def MM(out, lhsT, rhs, start, stop):
    return lambda e: e.matmul(out=out, lhsT=lhsT, rhs=rhs, start=start, stop=stop)


def TRN(out, a, idn):
    return lambda e: e.transpose(out=out, in_=a, identity=idn)


def ACTF(out, a, func, bias=None, scale=1.0):
    if bias is None:
        return lambda e: e.activation(out=out, in_=a, func=func, scale=scale)
    return lambda e: e.activation(out=out, in_=a, func=func, bias=bias, scale=scale)


def ACP(out, a):
    return lambda e: e.copy(out=out, in_=a)


def RED(out, a, op):
    return lambda e: e.tensor_reduce(out=out, in_=a, axis=AX.X, op=op)


def TTSC(out, d0, d1, init, op0, op1):
    return lambda e: e.tensor_tensor_scan(out=out, data0=d0, data1=d1, initial=init,
                                          op0=op0, op1=op1)


def DMA(out, a, slow=False):
    if slow:
        return lambda e: e.dma_start(out=out, in_=a, allow_slow_non_contiguous=True)
    return lambda e: e.dma_start(out=out, in_=a)


def build_program():
    nc = bass.Bass("TRN2", target_bir_lowering=False,
                   dynamic_dma_scratch_size=16384)
    v = Env()
    v.nc = nc
    sch = Sched()
    v.sch = sch
    add = sch.add
    di = {}

    def din(name, shape, dtype=F32):
        di[name] = nc.dram_tensor(name, shape, dtype, kind="ExternalInput").ap()

    for nm, shp in [("x", [TOK, D]), ("W_in", [D, 2048]), ("W_out", [D, D]),
                    ("WqkvT", [D, 1536]), ("Wout_lT", [512, D]), ("WqT", [D, 512]),
                    ("WkvT", [D, 1024]), ("Wout_hT", [512, D]), ("A_t", [P, 8]),
                    ("gamma_t", [P, 8]), ("beta_t", [P, 8]), ("lnw_t", [P, 8]),
                    ("lnb_t", [P, 8]), ("ffng_t", [P, 8]), ("b_in_t", [P, 16]),
                    ("b_out_row", [1, D]), ("null_q_row", [1, D]),
                    ("null_kv", [2 * H * DH]), ("rt2", [D, 2]), ("ones1", [1, 512]),
                    ("ident", [P, P]), ("ltri", [P, P]), ("onesP", [P, P]),
                    ("cidx", [P, 64]), ("kvec", [P, 4]), ("logk", [P, 4]),
                    ("na0", [P, 4]), ("consts", [P, 8])]:
        din(nm, shp)
    din("identb", [P, P], BF16)
    din("IOTA", [P, P])
    din("pcol", [P, 1])
    din("maskLt", [P, P], BF16)
    din("maskRt", [P, P], BF16)
    v.di = di

    v.xn_dram = nc.dram_tensor("xn_scr", [TOK, D], F32, kind="Internal").ap()
    v.sp_dram = nc.dram_tensor("sp_scr", [TOK, D], F32, kind="Internal").ap()
    v.light_dram = nc.dram_tensor("light_scr", [TOK, D], F32, kind="Internal").ap()
    v.ffn_dram = nc.dram_tensor("ffn_scr", [TOK, D], F32, kind="Internal").ap()
    v.heavy_dram = nc.dram_tensor("heavy_scr", [B * 520 + 8, D], F32, kind="Internal").ap()
    v.sc_bounce = nc.dram_tensor("sc_bounce", [2, TOK], F32, kind="Internal").ap()
    v.idx_dram = nc.dram_tensor("idx_scr", [B * (NQ + NKV)], I32, kind="Internal").ap()
    v.kvm_dram = nc.dram_tensor("kvm_scr", [B * N], F32, kind="Internal").ap()
    v.rnk_dram = nc.dram_tensor("rnk_scr", [B * N], F32, kind="Internal").ap()
    v.out_d = nc.dram_tensor("outp", [TOK, D], F32, kind="ExternalOutput").ap()
    v.dbg = {}
    for name, shape, _fn in TAPS:
        v.dbg[name] = nc.dram_tensor("dbg_" + name, list(shape), F32,
                                     kind="ExternalOutput").ap()

    stack = ExitStack()
    with stack:
        sbmap = {}

        def sbuf(name, shape, dtype=F32):
            t = stack.enter_context(nc.sbuf_tensor(name + "_s", shape, dtype))
            sbmap[name] = t
            return t

        v.sb = sbmap

        AR_COLS = 47500
        arena = sbuf("arena", [P, AR_COLS])

        def arf(off_b, cols):
            assert off_b % 4 == 0 and off_b // 4 + cols <= AR_COLS, (off_b, cols)
            return arena[:, off_b // 4: off_b // 4 + cols]

        def arb(off_b, cols):
            assert off_b % 4 == 0 and off_b // 4 + (cols + 1) // 2 <= AR_COLS
            return arena[:, off_b // 4: off_b // 4 + (cols + 1) // 2].bitcast(BF16)[:, 0:cols]

        yT = [arb(dg * 8192, TOK) for dg in range(8)]
        WA = 65536
        winb = [arb(WA + i * 4096, 2048) for i in range(8)]
        woutb = [arb(WA + 32768 + i * 2048, D) for i in range(8)]
        wqkvb = [arb(WA + i * 3072, 1536) for i in range(8)]
        woutlb = [arb(WA + 24576 + i * 2048, D) for i in range(4)]
        wqb = [arb(WA + i * 1024, 512) for i in range(8)]
        wkvb = [arb(WA + 8192 + i * 2048, 1024) for i in range(8)]
        wouthb = [arb(WA + 24576 + i * 2048, D) for i in range(4)]
        R3 = WA + 49152
        xts = [arf(R3 + 16384 + i * 4096, D) for i in range(2)]
        xnt = [arf(R3 + 24576 + i * 4096, D) for i in range(2)]
        ybt = [arb(R3 + 32768 + i * 2048, D) for i in range(2)]
        null_q_rep = arf(R3 + 40960, D)
        null_q_row = arf(R3 + 45056, D)[0:1, :]
        wstg = [arf(R3 + i * 8192, 2048) for i in range(2)]
        TCB = 16 * TCP1 * 4
        u_pl = arf(R3, 16 * TCP1)
        g_pl = arf(R3 + TCB, 16 * TC)
        Q_pl = arf(R3 + TCB + 4 * 16 * TC, 16 * TC)
        gm_pl = arf(R3 + TCB + 8 * 16 * TC, 16 * TC)
        cC_pl = arf(R3 + TCB + 12 * 16 * TC, 16 * TC)
        mq_pl = arf(R3 + TCB + 16 * 16 * TC, 16 * TC)
        h_pl = arf(R3 + TCB + 20 * 16 * TC, 16 * TC)
        PHS = R3 + TCB + 24 * 16 * TC
        hsb_pl = arb(PHS, 16 * TC)
        pjout = [arf(PHS + 4096 + i * 2048, 512) for i in range(2)]
        qbuf = arf(PHS + 12288, 64)
        zTc = [[arb(PHS + 12544 + (b2 * 8 + dg) * 264, TCP1) for dg in range(8)]
               for b2 in range(B)]
        gtmp = [arf(PHS + 16768 + i * 512, TC) for i in range(2)]
        qkpl = arb(R3, 16 * 1024)
        vaug_l = [arb(R3 + 32768 + j * 1040, H * 65) for j in range(16)]
        loTp = [arb(R3 + 49408 + ic * 4096, 16 * P) for ic in range(4)]
        z2Tc = [arb(R3 + 65792 + dg * 256, P) for dg in range(8)]
        kroll = [arb(R3 + 67840 + j3 * 256, P) for j3 in range(3)]
        qroll = arb(R3 + 68608, P)
        Pt_l = [arb(R3 + 68864 + j3 * 256, P) for j3 in range(3)]
        o_sb = arb(R3 + 69632, P)
        recip_r = arf(R3 + 69888, P)
        lout = arf(R3 + 70400, D)
        W6 = WA + 32768
        xqT = [arb(W6 + dg * 1024, NQ) for dg in range(8)]
        qT_hb = arb(W6 + 8192, NQ)
        kT_hb = arb(W6 + 9216, N)
        denr = arf(W6 + 13312, 512)
        xkvT = [arb(R3 + dg * 4096, N) for dg in range(8)]
        gat = [arf(R3 + 32768 + i * 4096, D) for i in range(2)]
        gatb = arb(R3 + 40960, D)
        vaug_h = [arb(R3 + 43008 + j * 1040, H * 65) for j in range(16)]
        nullva = arf(R3 + 59648, 520)[0:1, :]
        nullvab = arb(R3 + 61952, H * 65)[0:1, :]
        Pn_h = arb(R3 + 63232, NQ)
        Pt_h = arb(R3 + 64256, NQ)
        hoT = [arb(R3 + 65280 + ic * 1024, NQ) for ic in range(4)]
        o_sbh = arb(R3 + 69376, NQ)
        hvout = arf(R3 + 70400, D)
        f_x = [arf(R3 + i * 4096, D) for i in range(2)]
        f_xn = [arf(R3 + 8192 + i * 4096, D) for i in range(2)]
        f_sp = [arf(R3 + 16384 + i * 4096, D) for i in range(2)]
        f_li = [arf(R3 + 24576 + i * 4096, D) for i in range(2)]
        f_ff = [arf(R3 + 32768 + i * 4096, D) for i in range(2)]
        f_o = [arf(R3 + 40960 + i * 4096, D) for i in range(2)]
        f_t = [arf(R3 + 49152 + i * 4096, D) for i in range(4)]

        A_t = sbuf("A_t", [P, 8]); gamma_t = sbuf("gamma_t", [P, 8])
        beta_t = sbuf("beta_t", [P, 8]); lnw_t = sbuf("lnw_t", [P, 8])
        lnb_t = sbuf("lnb_t", [P, 8]); ffng_t = sbuf("ffng_t", [P, 8])
        b_in_t = sbuf("b_in_t", [P, 16]); halfb = sbuf("halfb", [P, 8])
        nA2 = sbuf("nA2", [P, 8]); A2 = sbuf("A2", [P, 8])
        b_out_row = sbuf("b_out_row", [1, D])
        rt2c = [sbuf(f"rt2c{i}", [P, 2]) for i in range(8)]
        ones1 = sbuf("ones1", [1, 512]); ident = sbuf("ident", [P, P])
        identb = sbuf("identb", [P, P], BF16); ltri = sbuf("ltri", [P, P])
        onesP = sbuf("onesP", [P, P])
        cidx = sbuf("cidx", [P, 64]); kvec = sbuf("kvec", [P, 4])
        logk = sbuf("logk", [P, 4]); na0_t = sbuf("na0_t", [P, 4])
        maskLt = sbuf("maskLt", [P, P], BF16); maskRt = sbuf("maskRt", [P, P], BF16)
        consts = sbuf("consts", [P, 8])
        nullkT = sbuf("nullkT", [DH, H]); nullkTb = sbuf("nullkTb", [DH, H], BF16)
        IOTA = sbuf("IOTA", [P, P]); pcol = sbuf("pcol", [P, 1])
        rankg = [sbuf(f"rankg{b2}", [P, 16]) for b2 in range(B)]
        cmpQ = sbuf("cmpQ", [P, P]); tokcol = sbuf("tokcol", [P, 1])
        rks = sbuf("rks", [P, 1])
        xnTf = [sbuf(f"xnTf{i}", [P, P]) for i in range(2)]
        sc_tile = [sbuf(f"sc_tile{i}", [2, P]) for i in range(2)]
        stats = sbuf("stats", [P, 24]); mv = sbuf("mv", [P, 4])
        st_rs = sbuf("st_rs", [P, 4])
        S_t = sbuf("S_t", [P, 64]); na_t = sbuf("na_t", [P, 4])
        T_t = sbuf("T_t", [P, 64]); E_t = sbuf("E_t", [P, 64])
        part_t = sbuf("part_t", [P, 4]); l_t = sbuf("l_t", [P, 4])
        tmp4 = [sbuf(f"tmp4_{i}", [P, 4]) for i in range(6)]
        qv_t = sbuf("qv_t", [P, 64]); e1_t = sbuf("e1_t", [P, 64])
        mkc_t = sbuf("mkc_t", [P, 64]); key_t = sbuf("key_t", [P, 64])
        lo_t = sbuf("lo_t", [P, 4]); hi_t = sbuf("hi_t", [P, 4])
        mid_t = sbuf("mid_t", [P, 4]); cmp_t = sbuf("cmp_t", [P, 64])
        ge_t = sbuf("ge_t", [P, 4]); mask_t = sbuf("mask_t", [P, 64])
        cs_t = sbuf("cs_t", [P, 64]); tot_t = sbuf("tot_t", [P, 4])
        rank_t = sbuf("rank_t", [P, 64]); rk2_t = sbuf("rk2_t", [P, 64])
        s64 = [sbuf(f"s64_{i}", [P, 64]) for i in range(4)]
        perm_t = sbuf("perm_t", [P, 64]); permi_t = sbuf("permi_t", [P, 64], I32)
        toki_t = sbuf("toki_t", [P, 16], I32)
        idx_q = [sbuf(f"idx_q{b2}", [P, NQ // P], I32) for b2 in range(B)]
        qidx_t = sbuf("qidx_t", [P, NTILE], I32)
        qstg = sbuf("qstg", [P, NTILE]); qstg2 = sbuf("qstg2", [P, NTILE])
        idx_kv = [sbuf(f"idx_kv{b2}", [P, NKV // P], I32) for b2 in range(B)]
        kvmask = [sbuf(f"kvmask{b2}", [P, 16]) for b2 in range(B)]
        ones16 = sbuf("ones16", [P, 16])
        wbuf = sbuf("wbuf", [P, 32]); hcar = sbuf("hcar", [P, 16])
        p1t = sbuf("p1t", [P, 16]); pct = sbuf("pct", [P, 16]); m1t = sbuf("m1t", [P, 16])
        spacer_d = sbuf("spacer_d", [P, 4])
        spacer_s = sbuf("spacer_s", [P, 4])

        def _mk_spacer(eng):
            if eng == "ve":
                return lambda e: e.tensor_copy(out=spacer_d[:], in_=spacer_s[:])
            if eng == "ac":
                return lambda e: e.copy(out=spacer_d[:, 0:1], in_=spacer_s[:, 0:1])
            return lambda e: e.tensor_copy(out=spacer_d[:], in_=spacer_s[:])

        sch.spacer_fn = _mk_spacer

        ps = [stack.enter_context(nc.psum_tensor(f"ps{i}", [P, 512], F32))
              for i in range(8)]

        v.locals = dict(locals())

        def nab(ap4):
            return ap4.rearrange("p (g o) -> p g o", o=1).broadcast_to([P, 4, 16])

        def g3(ap64):
            return ap64.rearrange("p (g o) -> p g o", o=16)

        def lane_ap(plane, lane, tcap, t0, n):
            return plane[:, lane * tcap + t0: lane * tcap + t0 + n]

        def col_ap(plane, tcap, t):
            return plane.rearrange("p (l tt) -> p l tt", tt=tcap)[:, :, t]

        for t, nm in [(A_t, "A_t"), (gamma_t, "gamma_t"), (beta_t, "beta_t"),
                      (lnw_t, "lnw_t"), (lnb_t, "lnb_t"), (ffng_t, "ffng_t"),
                      (b_in_t, "b_in_t"), (b_out_row, "b_out_row"),
                      (null_q_row, "null_q_row"), (ones1, "ones1"), (ident, "ident"),
                      (identb, "identb"), (ltri, "ltri"), (onesP, "onesP"),
                      (cidx, "cidx"), (kvec, "kvec"), (logk, "logk"), (na0_t, "na0"),
                      (consts, "consts"), (maskLt, "maskLt"), (maskRt, "maskRt"),
            (IOTA, "IOTA"), (pcol, "pcol")]:
            add("sy", DMA(t[:], di[nm]), writes=[nm], dma=True, tag=f"ld_{nm}")
        for i in range(8):
            add("sy", DMA(rt2c[i][:], di["rt2"][i * P:(i + 1) * P, :]),
                writes=[f"rt2c{i}"], dma=True, tag=f"ld_rt{i}")
        add("sy", DMA(nullkT[:], di["null_kv"].rearrange(
            "(kv h dh) -> dh (kv h)", kv=2, h=H)[:, 0:H], slow=True),
            writes=["nullkT"], dma=True, tag="ld_nullk")

        add("ve", CP(nullkTb[:], nullkT[:]), reads=["nullkT"], writes=["nullkTb"], tag="cnullk")
        add("gp", lambda e: e.iota(out=toki_t[:], pattern=[[1, 16]], base=0,
                                   channel_multiplier=16),
            writes=["toki"], tag="iota")
        add("ve", TS(halfb[:], b_in_t[:, 8:16], 0.5, OP.mult),
            reads=["b_in_t"], writes=["halfb"], tag="halfb")
        add("ve", TS(A2[:], A_t[:], 0.5, OP.mult), reads=["A_t"], writes=["A2"], tag="A2")
        add("ve", TS(nA2[:], A_t[:], -0.5, OP.mult), reads=["A_t"], writes=["nA2"], tag="nA2")
        add("ve", MS(ones16[:], 1.0), writes=["ones16"], tag="ones16")
        for b2 in range(B):
            add("sy", DMA(v.heavy_dram[b2 * 520 + 512: b2 * 520 + 513, :],
                          null_q_row[0:1, :]),
                reads=["null_q_row"], writes=[f"hnull{b2}"], dma=True, tag=f"hnul{b2}")

        for i in range(NTILE):
            bi = i % 2
            add("sy", DMA(xts[bi][:], di["x"][i * P:(i + 1) * P, :]),
                writes=[f"xs{bi}"], dma=True, tag=f"ldx{i}")
            add("ve", lambda e, bi=bi: e.bn_stats(out=stats[:, bi * 12:bi * 12 + 6],
                                                  in_=xts[bi][:, 0:512]),
                reads=[f"xs{bi}"], writes=[f"st{bi}a"], tag=f"bn1_{i}")
            add("ve", lambda e, bi=bi: e.bn_stats(out=stats[:, bi * 12 + 6:bi * 12 + 12],
                                                  in_=xts[bi][:, 512:1024]),
                reads=[f"xs{bi}"], writes=[f"st{bi}b"], tag=f"bn1b_{i}")
            add("ve", CP(tmp4[5][:], kvec[:]), reads=["kvec"], writes=[], tag=f"s1a{i}")
            add("ve", CP(tmp4[5][:], logk[:]), reads=["logk"], writes=[], tag=f"s1b{i}")
            add("ve", lambda e, bi=bi: e.bn_aggr(out=mv[:, bi * 2:bi * 2 + 2],
                                                 in_=stats[:, bi * 12:bi * 12 + 12]),
                reads=[f"st{bi}a", f"st{bi}b"], writes=[f"mv{bi}"], tag=f"ag1_{i}")
            add("ac", ACTF(st_rs[:, bi:bi + 1], mv[:, bi * 2 + 1:bi * 2 + 2], AF.Ln,
                           bias=consts[:, 0:1]),
                reads=[f"mv{bi}", "consts"], writes=[f"rln{bi}"], tag=f"ln1_{i}")
            add("ac", ACP(xnTf[0][0:1, 0:1], consts[0:1, 2:3]), reads=["consts"],
                writes=[], tag=f"aspc{i}")
            add("ac", ACTF(st_rs[:, bi:bi + 1], st_rs[:, bi:bi + 1], AF.Exp, scale=-0.5),
                reads=[f"rln{bi}"], writes=[f"rs1{bi}"], tag=f"ex1_{i}")
            add("ve", lambda e, bi=bi: e.tensor_scalar(
                out=xnt[bi][:], in0=xts[bi][:], scalar1=mv[:, bi * 2:bi * 2 + 1],
                scalar2=st_rs[:, bi:bi + 1], op0=OP.subtract, op1=OP.mult),
                reads=[f"xs{bi}", f"mv{bi}", f"rs1{bi}"], writes=[f"xnt{bi}"], tag=f"xn{i}",
                haz_ok=[f"mv{bi}"])
            add("gp", DMA(v.xn_dram[i * P:(i + 1) * P, :], xnt[bi][:]),
                reads=[f"xnt{bi}"], writes=[f"xnst{i}"], dma=True, tag=f"stxn{i}")
            add("ve", CP(tmp4[5][:], kvec[:]), reads=["kvec"], writes=[], tag=f"s3a{i}")
            add("ve", CP(tmp4[5][:], logk[:]), reads=["logk"], writes=[], tag=f"s3b{i}")
            add("ve", lambda e, bi=bi: e.bn_stats(out=stats[:, bi * 12:bi * 12 + 6],
                                                  in_=xnt[bi][:, 0:512]),
                reads=[f"xnt{bi}"], writes=[f"st{bi}a"], tag=f"bn2_{i}")
            add("ve", lambda e, bi=bi: e.bn_stats(out=stats[:, bi * 12 + 6:bi * 12 + 12],
                                                  in_=xnt[bi][:, 512:1024]),
                reads=[f"xnt{bi}"], writes=[f"st{bi}b"], tag=f"bn2b_{i}")
            add("ve", CP(tmp4[5][:], kvec[:]), reads=["kvec"], writes=[], tag=f"s2a{i}")
            add("ve", CP(tmp4[5][:], logk[:]), reads=["logk"], writes=[], tag=f"s2b{i}")
            add("ve", lambda e, bi=bi: e.bn_aggr(out=mv[:, bi * 2:bi * 2 + 2],
                                                 in_=stats[:, bi * 12:bi * 12 + 12]),
                reads=[f"st{bi}a", f"st{bi}b"], writes=[f"mv{bi}"], tag=f"ag2_{i}")
            add("ac", ACTF(st_rs[:, 2 + bi:3 + bi], mv[:, bi * 2 + 1:bi * 2 + 2], AF.Ln,
                           bias=consts[:, 0:1]),
                reads=[f"mv{bi}", "consts"], writes=[f"rln2{bi}"], tag=f"ln2_{i}")
            add("ac", ACP(xnTf[0][0:1, 0:1], consts[0:1, 2:3]), reads=["consts"],
                writes=[], tag=f"aspc2{i}")
            add("ac", ACTF(st_rs[:, 2 + bi:3 + bi], st_rs[:, 2 + bi:3 + bi], AF.Exp, scale=-0.5),
                reads=[f"rln2{bi}"], writes=[f"rs2{bi}"], tag=f"ex2_{i}")
            add("ve", lambda e, bi=bi: e.tensor_scalar(
                out=ybt[bi][:], in0=xnt[bi][:], scalar1=mv[:, bi * 2:bi * 2 + 1],
                scalar2=st_rs[:, 2 + bi:3 + bi], op0=OP.subtract, op1=OP.mult),
                reads=[f"xnt{bi}", f"mv{bi}", f"rs2{bi}"], writes=[f"ybt{bi}"], tag=f"y{i}",
                haz_ok=[f"mv{bi}"])
            for dg in range(8):
                py = 2 if dg % 2 == 0 else 6
                px = 3 if dg % 2 == 0 else 5
                add("te", TRN(ps[py][:, 0:64].bitcast(BF16)[:, 0:P],
                              ybt[bi][:, dg * P:(dg + 1) * P], identb[:]),
                    reads=[f"ybt{bi}", "identb"], writes=[f"PS{py}"], tag=f"Ty{i}_{dg}")
                add("te", TRN(ps[px][:, 0:P], xnt[bi][:, dg * P:(dg + 1) * P], ident[:]),
                    reads=[f"xnt{bi}", "ident"], writes=[f"PS{px}"], tag=f"Tx{i}_{dg}")
                add("ve", CP(yT[dg][:, i * P:(i + 1) * P], ps[py][:, 0:64].bitcast(BF16)[:, 0:P]),
                    reads=[f"PS{py}"], writes=[f"yT{dg}_{i}"], tag=f"evy{i}_{dg}")
                add("ac", ACP(xnTf[dg % 2][:], ps[px][:, 0:P]),
                    reads=[f"PS{px}"], writes=[f"xf{dg % 2}"], tag=f"evx{i}_{dg}")
                add("te", MM(ps[4][0:2, 0:P], rt2c[dg][:], xnTf[dg % 2][:],
                             dg == 0, dg == 7),
                    reads=[f"xf{dg % 2}", f"rt2c{dg}"], writes=["PS4"], tag=f"sc{i}_{dg}")
            add("ve", CP(sc_tile[i % 2][:], ps[4][0:2, 0:P]),
                reads=["PS4"], writes=[f"sct{i % 2}"], tag=f"evsc{i}")
            add("gp", DMA(v.sc_bounce[0:2, i * P:(i + 1) * P], sc_tile[i % 2][:]),
                reads=[f"sct{i % 2}"], writes=[f"scb{i}"], dma=True, tag=f"stsc{i}")
        sch.barrier()

        add("sy", DMA(S_t.rearrange("p (r b cc) -> p r b cc", r=2, b=B, cc=16),
                      v.sc_bounce.rearrange("r (b p cc) -> r b p cc", b=B, p=P, cc=16)
                      .transpose([2, 0, 1, 3])),
            reads=[f"scb{i}" for i in range(NTILE)], writes=["S"], dma=True, tag="sc_in")
        add("ve", CP(na_t[:], na0_t[:]), reads=["na0"], writes=["na"], tag="na0")
        for it in range(NITERS - 1):
            add("ve", TT(g3(T_t), g3(S_t), nab(na_t), OP.min),
                reads=["S", "na"], writes=["T"], tag=f"ri{it}")
            add("ve", CP(tmp4[5][:], kvec[:]), reads=["kvec"], writes=[], tag=f"risp{it}")
            add("ve", CP(tmp4[5][:], logk[:]), reads=["logk"], writes=[], tag=f"risp2{it}")
            add("ve", TT(g3(T_t), g3(T_t), nab(na_t), OP.subtract),
                reads=["T", "na"], writes=["T"], tag=f"rin{it}", haz_ok=["T"]),
            add("ac", ACTF(E_t[:], T_t[:], AF.Exp), reads=["T"], writes=["E"], tag=f"re{it}")
            add("ve", RED(part_t[:], E_t.rearrange("p (g cc) -> p g cc", cc=16), OP.add),
                reads=["E"], writes=["part"], tag=f"rr{it}")
            add("te", MM(ps[3][:, 4:8], onesP[:], part_t[:], True, True),
                reads=["part", "onesP"], writes=["PS3"], tag=f"rt{it}")
            add("ac", ACTF(l_t[:], ps[3][:, 4:8], AF.Ln),
                reads=["PS3"], writes=["l"], tag=f"rl{it}")
            add("ve", TT(tmp4[0][:], l_t[:], logk[:], OP.subtract),
                reads=["l", "logk"], writes=["t0"], tag=f"ra{it}")
            add("ve", CP(tmp4[1][:], na_t[:]), reads=["na"], writes=["t1"], tag=f"rb{it}")
            add("ve", CP(tmp4[2][:], part_t[:]), reads=["part"], writes=["t2x"], tag=f"rc{it}")
            add("ve", TT(na_t[:], tmp4[0][:], tmp4[1][:], OP.add),
                reads=["t0", "t1"], writes=["na"], tag=f"rn{it}")
        add("ve", TT(g3(qv_t), g3(S_t), nab(na_t), OP.subtract),
            reads=["S", "na"], writes=["qv"], tag="kqv")
        add("ve", TS(mkc_t[:], qv_t[:], 0.0, OP.is_ge), reads=["qv"], writes=["mkc"], tag="kmk")
        add("ve", TS(T_t[:], qv_t[:], 0.0, OP.min), reads=["qv"], writes=["T"], tag="kmin")
        add("ac", ACTF(e1_t[:], T_t[:], AF.Exp), reads=["T"], writes=["e1"], tag="kexp")
        add("ve", TT(s64[0][:], cidx[:], e1_t[:], OP.subtract),
            reads=["cidx", "e1"], writes=["k0"], tag="kd")
        add("ve", CP(s64[1][:], mkc_t[:]), reads=["mkc"], writes=["k1"], tag="kc")
        add("ve", CP(key_t[:], e1_t[:]), reads=["e1"], writes=["key"], tag="kc2")
        add("ve", TT(s64[2][:], s64[1][:], s64[0][:], OP.mult),
            reads=["k0", "k1"], writes=["k2"], tag="km")
        add("ve", CP(tmp4[5][:], kvec[:]), reads=["kvec"], writes=[], tag="ksp")
        add("ve", TT(key_t[:], key_t[:], s64[2][:], OP.add),
            reads=["key", "k2"], writes=["key"], tag="ka")
        add("ve", MS(lo_t[:], 0.0), writes=["lo"], tag="blo")
        add("ve", MS(hi_t[:], 2.125), writes=["hi"], tag="bhi")
        for it in range(BISECT):
            add("ve", TT(mid_t[:], lo_t[:], hi_t[:], OP.add),
                reads=["lo", "hi"], writes=["midr"], tag=f"ba{it}")
            add("ve", CP(tmp4[3][:], lo_t[:]), reads=["lo"], writes=["lsv"], tag=f"bs{it}")
            add("ve", CP(tmp4[4][:], hi_t[:]), reads=["hi"], writes=["hsv"], tag=f"bh{it}")
            add("ve", TS(mid_t[:], mid_t[:], 0.5, OP.mult),
                reads=["midr"], writes=["mid"], tag=f"bm{it}")
            add("ve", TT(g3(cmp_t), g3(key_t), nab(mid_t), OP.is_gt),
                reads=["key", "mid"], writes=["cmp"], tag=f"bc{it}")
            add("ve", RED(part_t[:], cmp_t.rearrange("p (g cc) -> p g cc", cc=16), OP.add),
                reads=["cmp"], writes=["part"], tag=f"br{it}")
            add("te", MM(ps[3][:, 4:8], onesP[:], part_t[:], True, True),
                reads=["part", "onesP"], writes=["PS3"], tag=f"bt{it}")
            add("ve", TT(ge_t[:], ps[3][:, 4:8], kvec[:], OP.is_ge),
                reads=["PS3", "kvec"], writes=["ge"], tag=f"bg{it}")
            add("ve", TT(tmp4[0][:], mid_t[:], tmp4[3][:], OP.subtract),
                reads=["mid", "lsv"], writes=["t0"], tag=f"bd1{it}")
            add("ve", TT(tmp4[1][:], tmp4[4][:], mid_t[:], OP.subtract),
                reads=["hsv", "mid"], writes=["t1"], tag=f"bd2{it}")
            add("ve", TT(tmp4[0][:], tmp4[0][:], ge_t[:], OP.mult),
                reads=["t0", "ge"], writes=["t0b"], tag=f"bm1{it}")
            add("ve", TT(tmp4[1][:], tmp4[1][:], ge_t[:], OP.mult),
                reads=["t1", "ge"], writes=["t1b"], tag=f"bm2{it}")
            add("ve", TT(lo_t[:], tmp4[3][:], tmp4[0][:], OP.add),
                reads=["lsv", "t0b"], writes=["lo"], tag=f"blo{it}")
            add("ve", TT(hi_t[:], mid_t[:], tmp4[1][:], OP.add),
                reads=["mid", "t1b"], writes=["hi"], tag=f"bhi{it}")
        add("ve", TT(g3(mask_t), g3(key_t), nab(lo_t), OP.is_gt),
            reads=["key", "lo"], writes=["mask"], tag="msk")
        for rb in range(4):
            add("ve", TTSC(cs_t[:, rb * 16:rb * 16 + 16], ones16[:],
                           mask_t[:, rb * 16:rb * 16 + 16], 0.0, OP.mult, OP.add),
                reads=["mask", "ones16"], writes=[f"cs{rb}"], tag=f"cs{rb}")
            add("ve", CP(tmp4[5][:], kvec[:]), reads=["kvec"], writes=[], tag=f"cs{rb}a")
            add("ve", CP(tmp4[5][:], logk[:]), reads=["logk"], writes=[], tag=f"cs{rb}b")
        add("ve", CP(tot_t[:], cs_t.rearrange("p (g cc) -> p g cc", cc=16)[:, :, 15]),
            reads=["cs0", "cs1", "cs2", "cs3"], writes=["tot"], tag="tot")
        add("te", MM(ps[3][:, 8:12], ltri[:], tot_t[:], True, True),
            reads=["tot", "ltri"], writes=["PS3"], tag="pref")
        add("ve", TT(rank_t[:], cs_t[:], mask_t[:], OP.subtract),
            reads=["cs0", "cs1", "cs2", "cs3", "mask"], writes=["rk0"], tag="rk0")
        add("ve", CP(tmp4[5][:], kvec[:]), reads=["kvec"], writes=[], tag="rksp")
        add("ve", CP(s64[3][:], mask_t[:]), reads=["mask"], writes=["msv"], tag="rksp2")
        add("ve", TT(g3(rank_t), g3(rank_t), nab(ps[3][:, 8:12]), OP.add),
            reads=["rk0", "PS3"], writes=["rank"], tag="rk1")
        add("ve", TS(rk2_t[:], s64[3][:], -1e9, OP.mult, 1e9, OP.add),
            reads=["msv"], writes=["rkb"], tag="rk2")
        add("ve", CP(tmp4[5][:], logk[:]), reads=["logk"], writes=[], tag="rk5sp2")
        add("ve", TT(rk2_t[:], rank_t[:], rk2_t[:], OP.add),
            reads=["rank", "rkb"], writes=["rk"], tag="rk6")
        for b in range(B):
            add("gp", DMA(v.rnk_dram[b * N:(b + 1) * N].rearrange("(p cc) -> p cc", p=P),
                          rk2_t[:, b * 16:b * 16 + 16]),
                reads=["rk"], writes=[f"rkd{b}"], dma=True, tag=f"rko{b}")
        for b in range(B):
            add("sy", DMA(rankg[b][:],
                          v.rnk_dram[b * N:(b + 1) * N]
                          .rearrange("(tc p2) -> tc p2", p2=P).transpose([1, 0]), slow=True),
                reads=[f"rkd{b}"], writes=[f"rkg{b}"], dma=True, tag=f"rki{b}")
        for b in range(B):
            for jj in range(NQ // P):
                for tc in range(16):
                    add("ve", lambda e, b=b, tc=tc, jj=jj: e.tensor_scalar(
                        out=rks[:], in0=rankg[b][:, tc:tc + 1],
                        scalar1=float(-jj * P), scalar2=None, op0=OP.add),
                        reads=[f"rkg{b}"], writes=["rks"], tag=f"iq_s{b}{jj}{tc}")
                    add("ve", lambda e, tc=tc: e.tensor_scalar(
                        out=tokcol[:], in0=pcol[:], scalar1=float(tc * P),
                        scalar2=None, op0=OP.add),
                        reads=["pcol"], writes=["tokc"], tag=f"iq_t{b}{jj}{tc}")
                    add("ve", lambda e: e.tensor_scalar(
                        out=cmpQ[:], in0=IOTA[:], scalar1=rks[:, 0:1], scalar2=None,
                        op0=OP.is_equal),
                        reads=["rks", "IOTA"], writes=["cmpQ"], tag=f"iq_c{b}{jj}{tc}")
                    add("te", MM(ps[2][:, 0:1], cmpQ[:], tokcol[:], tc == 0, tc == 15),
                        reads=["cmpQ", "tokc"], writes=["PS2"], tag=f"iq_m{b}{jj}{tc}")
                add("ve", lambda e, b=b, jj=jj: e.tensor_copy(
                    out=idx_q[b][:, jj:jj + 1], in_=ps[2][:, 0:1]),
                    reads=["PS2"], writes=[f"ixq{b}"], tag=f"iq_e{b}{jj}")
        for b in range(B):
            add("gp", DMA(v.kvm_dram[b * N:(b + 1) * N].rearrange("(p cc) -> p cc", p=P),
                          mask_t[:, (2 + b) * 16:(2 + b) * 16 + 16]),
                reads=["mask"], writes=[f"mkd{b}"], dma=True, tag=f"mko{b}")
        for b in range(B):
            add("sy", DMA(kvmask[b][:],
                          v.kvm_dram[b * N:(b + 1) * N]
                          .rearrange("(jc p2) -> jc p2", p2=P).transpose([1, 0]), slow=True),
                reads=[f"mkd{b}"], writes=[f"kvm{b}"], dma=True, tag=f"mki{b}")
        for i in range(NTILE):
            add("sy", DMA(qstg[:, i:i + 1],
                          v.rnk_dram[i * P:(i + 1) * P].rearrange("(p o) -> p o", o=1)),
                reads=[f"rkd{i // 16}"], writes=[f"qst{i}"], dma=True, tag=f"qix{i}")
        add("ve", TS(qstg2[:], qstg[:], 512.0, OP.min),
            reads=[f"qst{i}" for i in range(NTILE)], writes=["qmn"], tag="qmn")
        add("ve", CP(qidx_t[:], qstg2[:]), reads=["qmn"], writes=["qidx"], tag="qcp")
        sch.barrier()
        if DEBUG_PHASES < 4:
            _finish_debug(v, add, sch, locals())
            sch.schedule(nc, stack)
            _emit_all(sch, nc, stack)
            return nc, v

        wstage = [(di["W_in"][i * P:(i + 1) * P, :], winb[i], 2048, f"wI{i}") for i in range(8)]
        wstage += [(di["W_out"][i * P:(i + 1) * P, :], woutb[i], D, f"wO{i}") for i in range(8)]
        for si, (srcw, dst, cols, wk) in enumerate(wstage):
            bi = si % 2
            add("sy", DMA(wstg[bi][:, 0:cols], srcw), writes=[f"wg{bi}"], dma=True,
                tag=f"wl{si}")
            add("ve", CP(dst[:, 0:cols], wstg[bi][:, 0:cols]),
                reads=[f"wg{bi}"], writes=[wk], tag=f"wc{si}")
        sch.barrier()

        for k in range(min(NCHUNK, P4_NCH)):
            ucols = TCP1 if k < NCHUNK - 1 else TC
            for bb in range(B):
                for dg in range(8):
                    add("ve", lambda e, bb=bb, dg=dg, k=k, ucols=ucols: e.tensor_scalar(
                        out=zTc[bb][dg][:, 0:ucols],
                        in0=yT[dg][:, bb * N + k * TC: bb * N + k * TC + ucols],
                        scalar1=gamma_t[:, dg:dg + 1], scalar2=beta_t[:, dg:dg + 1],
                        op0=OP.mult, op1=OP.add),
                        reads=["gamma_t", "beta_t"], writes=[f"z{bb}_{dg}"], tag=f"z{k}_{bb}{dg}")
            for gi in range(min(32, P4_NGI)):
                side, rem = divmod(gi, 16)
                bb, ec = divmod(rem, 8)
                lane = ec * 2 + bb
                bank = gi % 8
                fcols = ucols if side == 0 else TC
                pk = f"PS{bank}"
                for dg in range(8):
                    add("te", MM(ps[bank][:, 0:fcols],
                                 winb[dg][:, (side * 8 + ec) * P:(side * 8 + ec) * P + P],
                                 zTc[bb][dg][:, 0:fcols], dg == 0, dg == 7),
                        reads=[f"wI{dg}", f"z{bb}_{dg}"], writes=[pk], tag=f"ug{k}_{gi}_{dg}")
                if side == 0:
                    add("ve", lambda e, lane=lane, bank=bank, ec=ec, fcols=fcols:
                        e.tensor_scalar(
                            out=lane_ap(u_pl, lane, TCP1, 0, fcols),
                            in0=ps[bank][:, 0:fcols],
                            scalar1=b_in_t[:, ec:ec + 1], scalar2=None, op0=OP.add),
                        reads=[pk, "b_in_t"], writes=[f"u{lane}"], tag=f"ue{k}_{gi}")
                else:
                    gb = gi % 2
                    add("ac", ACTF(gtmp[gb][:], ps[bank][:, 0:TC],
                                   AF.Tanh, bias=halfb[:, ec:ec + 1], scale=0.5),
                        reads=[pk, "halfb"], writes=[f"gt{gb}"], tag=f"gt{k}_{gi}")
                    add("ve", lambda e, gb=gb, lane=lane: e.tensor_scalar(
                        out=lane_ap(g_pl, lane, TC, 0, TC), in0=gtmp[gb][:],
                        scalar1=0.5, scalar2=0.5, op0=OP.mult, op1=OP.add),
                        reads=[f"gt{gb}"], writes=[f"g{lane}"], tag=f"ge{k}_{gi}")
                    add("ve", lambda e, gb=gb, lane=lane, ec=ec: e.tensor_scalar(
                        out=lane_ap(Q_pl, lane, TC, 0, TC), in0=gtmp[gb][:],
                        scalar1=nA2[:, ec:ec + 1], scalar2=A2[:, ec:ec + 1],
                        op0=OP.mult, op1=OP.add),
                        reads=[f"gt{gb}", "nA2", "A2"], writes=[f"Q{lane}"], tag=f"Qe{k}_{gi}")
                    add("ve", lambda e, gb=gb, lane=lane: e.tensor_scalar(
                        out=lane_ap(gm_pl, lane, TC, 0, TC), in0=gtmp[gb][:],
                        scalar1=-0.5, scalar2=0.5, op0=OP.mult, op1=OP.add),
                        reads=[f"gt{gb}"], writes=[f"gm{lane}", f"gt{gb}"], tag=f"gme{k}_{gi}")
            if P4_NGI < 32:
                continue
            ur = [f"u{l}" for l in range(16)]
            gr = [f"g{l}" for l in range(16)]
            if k == NCHUNK - 1:
                add("ve", MS(col_ap(u_pl, TCP1, TC), 0.0), reads=ur, writes=ur,
                    tag=f"up{k}")
            add("ve", TT(cC_pl.rearrange("p (l tt) -> p l tt", tt=TC),
                         g_pl.rearrange("p (l tt) -> p l tt", tt=TC),
                         u_pl.rearrange("p (l tt) -> p l tt", tt=TCP1)[:, :, 0:TC],
                         OP.mult),
                reads=ur + gr, writes=["cgu"], tag=f"cg{k}")
            add("ve", CP(tmp4[5][:], kvec[:]), reads=["kvec"], writes=[], tag=f"cgs{k}")
            add("ve", CP(tmp4[5][:], logk[:]), reads=["logk"], writes=[], tag=f"cgs2{k}")
            add("ve", TT(cC_pl.rearrange("p (l tt) -> p l tt", tt=TC),
                         u_pl.rearrange("p (l tt) -> p l tt", tt=TCP1)[:, :, 1:TCP1],
                         cC_pl.rearrange("p (l tt) -> p l tt", tt=TC), OP.subtract),
                reads=["cgu"] + ur, writes=["cC"], tag=f"cc{k}")
            if k == 0:
                add("ve", CP(wbuf[:, 0:16], col_ap(u_pl, TCP1, 0)),
                    reads=ur, writes=["w0"], tag="winit")
                add("ve", MS(hcar[:], 0.0), writes=["hcar"], tag="hinit")
            cks = ["cC"] + gr + [f"Q{l}" for l in range(16)] + [f"gm{l}" for l in range(16)]
            for t in range(TC if P4_PARTS >= 2 else 0):
                tg = k * TC + t
                cur = wbuf[:, (tg % 2) * 16:(tg % 2) * 16 + 16]
                nxt = wbuf[:, ((tg + 1) % 2) * 16:((tg + 1) % 2) * 16 + 16]
                wk_, nk_ = f"w{tg % 2}", f"w{(tg + 1) % 2}"
                ex = cks if (t == 0 or t == TC - 1) else []
                add("ac", ACTF(qbuf[:, (t % 4) * 16:(t % 4) * 16 + 16], cur, AF.Tanh),
                    reads=[wk_] + ex + [f"qf{t % 4}"], writes=[f"q{t}"], tag=f"th{tg}")
                add("gp", TT(p1t[:], col_ap(g_pl, TC, t), cur, OP.mult),
                    reads=[wk_] + ex, writes=["p1"], tag=f"p1{tg}")
                add("gp", TT(pct[:], p1t[:], col_ap(cC_pl, TC, t), OP.add),
                    reads=["p1"], writes=["pc"], tag=f"pc{tg}")
                add("ve", TT(m1t[:], col_ap(Q_pl, TC, t),
                             qbuf[:, (t % 4) * 16:(t % 4) * 16 + 16], OP.mult),
                    reads=[f"q{t}"] + ex, writes=["m1"], tag=f"m1{tg}")
                add("ve", TT(col_ap(mq_pl, TC, t), col_ap(gm_pl, TC, t),
                             qbuf[:, (t % 4) * 16:(t % 4) * 16 + 16], OP.mult),
                    reads=[f"q{t}"], writes=[f"mq{t}", f"qf{t % 4}"], tag=f"mq{tg}")
                add("ve", CP(tmp4[5][:], kvec[:]), reads=["kvec"], writes=[],
                    tag=f"ws{tg}")
                add("ve", TT(nxt, pct[:], m1t[:], OP.add),
                    reads=["pc", "m1"], writes=[nk_], tag=f"wn{tg}")
            mqr = [f"mq{t}" for t in range(TC - 3, TC)]
            for lane in range(16 if P4_PARTS >= 3 else 0):
                add("ve", TTSC(lane_ap(h_pl, lane, TC, 0, TC),
                               lane_ap(g_pl, lane, TC, 0, TC),
                               lane_ap(mq_pl, lane, TC, 0, TC),
                               hcar[:, lane:lane + 1], OP.mult, OP.add),
                    reads=(mqr + gr + [f"mq{t}" for t in range(0, TC, 32)]
                           if lane == 0 else []) + ["hcar", f"g{lane}", f"mq{TC - 1}"],
                    writes=[f"h{lane}"], tag=f"tt{k}_{lane}")
                add("ve", CP(tmp4[5][:], kvec[:]), reads=["kvec"], writes=[],
                    tag=f"tta{k}{lane}")
                add("ve", CP(tmp4[5][:], logk[:]), reads=["logk"], writes=[],
                    tag=f"ttb{k}{lane}")
            if P4_PARTS < 3:
                continue
            add("ve", CP(hcar[:], col_ap(h_pl, TC, TC - 1)),
                reads=[f"h{l}" for l in range(16)], writes=["hcar"], tag=f"hc{k}")
            add("ve", CP(hsb_pl[:], h_pl[:, 0:16 * TC]),
                reads=[f"h{l}" for l in range(16)], writes=["hsb"], tag=f"hs{k}")
            for bb in range(B if P4_PARTS >= 4 else 0):
                for eh in range(2):
                    pb = 2 + eh
                    pk = f"PS{pb}"
                    for dg in range(8):
                        add("te", MM(ps[pb][:, 0:512],
                                     hsb_pl.rearrange("p (l tt) -> p l tt", tt=TC)
                                     [:, dg * 2 + bb, :],
                                     woutb[dg][:, eh * 512:eh * 512 + 512],
                                     dg == 0, False),
                            reads=["hsb", f"wO{dg}"], writes=[pk], tag=f"pj{k}_{bb}{eh}_{dg}")
                    add("te", MM(ps[pb][0:TC, 0:512], ones1[0:1, 0:TC],
                                 b_out_row[0:1, eh * 512:eh * 512 + 512], False, True),
                        reads=["ones1", "b_out_row"], writes=[pk], tag=f"pjb{k}_{bb}{eh}")
                    add("ve", CP(pjout[eh][:, 0:512], ps[pb][:, 0:512]),
                        reads=[pk], writes=[f"po{eh}"], tag=f"pe{k}_{bb}{eh}")
                    add("gp", DMA(v.sp_dram[bb * N + k * TC: bb * N + k * TC + TC,
                                            eh * 512:eh * 512 + 512],
                                  pjout[eh][0:TC, 0:512]),
                        reads=[f"po{eh}"], writes=[f"sps{k}{bb}{eh}"], dma=True,
                        tag=f"ss{k}_{bb}{eh}")
        sch.barrier()
        if DEBUG_PHASES < 5:
            _finish_debug(v, add, sch, locals())
            sch.schedule(nc, stack)
            _emit_all(sch, nc, stack)
            return nc, v

        wstage = [(di["WqkvT"][i * P:(i + 1) * P, :], wqkvb[i], 1536, f"wQ{i}")
                  for i in range(8)]
        wstage += [(di["Wout_lT"][i * P:(i + 1) * P, :], woutlb[i], D, f"wL{i}")
                   for i in range(4)]
        for si, (srcw, dst, cols, wk) in enumerate(wstage):
            bi = si % 2
            add("sy", DMA(wstg[bi][:, 0:cols], srcw), writes=[f"wg{bi}"], dma=True,
                tag=f"w5l{si}")
            add("ve", CP(dst[:, 0:cols], wstg[bi][:, 0:cols]),
                reads=[f"wg{bi}"], writes=[wk], tag=f"w5c{si}")
        sch.barrier()

        for b in range(B):
            for tt in range(16):
                for dg in range(8):
                    add("ve", lambda e, b=b, tt=tt, dg=dg: e.tensor_scalar(
                        out=z2Tc[dg][:, 0:P],
                        in0=yT[dg][:, b * N + tt * P: b * N + tt * P + P],
                        scalar1=lnw_t[:, dg:dg + 1], scalar2=lnb_t[:, dg:dg + 1],
                        op0=OP.mult, op1=OP.add),
                        reads=["lnw_t", "lnb_t"], writes=[f"z2_{dg}"], tag=f"z2{b}_{tt}{dg}")
                for ec in range(3):
                    pb = 2 + (ec % 2)
                    pk = f"PS{pb}"
                    for dg in range(8):
                        add("te", MM(ps[pb][:, 0:512], z2Tc[dg][:, 0:P],
                                     wqkvb[dg][:, ec * 512:ec * 512 + 512], dg == 0, dg == 7),
                            reads=[f"z2_{dg}", f"wQ{dg}"], writes=[pk],
                            tag=f"qkv{b}_{tt}{ec}_{dg}")
                    if ec < 2:
                        add("ve", CP(qkpl[:, tt * 1024 + ec * 512: tt * 1024 + ec * 512 + 512],
                                     ps[pb][:, 0:512]),
                            reads=[pk], writes=[f"qk{tt}_{ec}"], tag=f"qke{b}_{tt}{ec}")
                    else:
                        add("ve", CP(vaug_l[tt].rearrange("p (h e) -> p h e", e=65)[:, :, 0:64],
                                     ps[pb][:, 0:512].rearrange("p (h e) -> p h e", e=64)),
                            reads=[pk], writes=[f"va{tt}"], tag=f"ve{b}_{tt}")
                add("ve", MS(vaug_l[tt].rearrange("p (h e) -> p h e", e=65)[:, :, 64:65], 1.0),
                    reads=[f"va{tt}"], writes=[f"va{tt}"], tag=f"vo{b}_{tt}")
            for h in range(H):
                for isl in range(16):
                    jts = [j for j in (isl - 1, isl, isl + 1) if 0 <= j < 16]
                    add("te", TRN(ps[4][0:64, 0:64].bitcast(BF16)[:, 0:P],
                                  qkpl[:, isl * 1024 + h * 64: isl * 1024 + h * 64 + 64],
                                  identb[:]),
                        reads=[f"qk{isl}_0", "identb"], writes=["PS4"], tag=f"Tq{b}{h}_{isl}")
                    add("ve", CP(qroll[0:64, 0:P], ps[4][0:64, 0:64].bitcast(BF16)[:, 0:P]),
                        reads=["PS4"], writes=["qroll"], tag=f"eq{b}{h}_{isl}")
                    for j3, jt in enumerate(jts):
                        add("te", TRN(ps[5][0:64, 0:64].bitcast(BF16)[:, 0:P],
                                      qkpl[:, jt * 1024 + 512 + h * 64:
                                           jt * 1024 + 512 + h * 64 + 64], identb[:]),
                            reads=[f"qk{jt}_1", "identb"], writes=["PS5"],
                            tag=f"Tk{b}{h}_{isl}{j3}")
                        add("ve", CP(kroll[j3][0:64, 0:P],
                                     ps[5][0:64, 0:64].bitcast(BF16)[:, 0:P]),
                            reads=["PS5"], writes=[f"kr{j3}"], tag=f"ek{b}{h}_{isl}{j3}")
                        add("te", MM(ps[6][:, 0:P], kroll[j3][0:64, 0:P],
                                     qroll[0:64, 0:P], True, True),
                            reads=[f"kr{j3}", "qroll"], writes=["PS6"],
                            tag=f"sm{b}{h}_{isl}{j3}")
                        add("ac", ACTF(Pt_l[j3][:, 0:P], ps[6][:, 0:P], AF.Exp, scale=0.125),
                            reads=["PS6"], writes=[f"P{j3}"], tag=f"ex{b}{h}_{isl}{j3}")
                        if jt == isl - 1:
                            add("ve", TT(Pt_l[j3][:, 0:P], Pt_l[j3][:, 0:P], maskLt[:],
                                         OP.mult),
                                reads=[f"P{j3}", "maskLt"], writes=[f"P{j3}"],
                                tag=f"mL{b}{h}_{isl}{j3}")
                        elif jt == isl + 1:
                            add("ve", TT(Pt_l[j3][:, 0:P], Pt_l[j3][:, 0:P], maskRt[:],
                                         OP.mult),
                                reads=[f"P{j3}", "maskRt"], writes=[f"P{j3}"],
                                tag=f"mR{b}{h}_{isl}{j3}")
                    for j3, jt in enumerate(jts):
                        add("te", MM(ps[7][0:65, 0:P], vaug_l[jt][:, h * 65:h * 65 + 65],
                                     Pt_l[j3][:, 0:P], j3 == 0, j3 == len(jts) - 1),
                            reads=[f"P{j3}", f"va{jt}"], writes=["PS7"],
                            tag=f"oT{b}{h}_{isl}{j3}")
                    add("ac", ACTF(recip_r[0:1, 0:P], ps[7][64:65, 0:P], AF.Ln),
                        reads=["PS7"], writes=["rcl"], tag=f"rcl{b}{h}_{isl}")
                    add("ac", ACP(xnTf[0][0:1, 0:1], consts[0:1, 2:3]), reads=["consts"],
                        writes=[], tag=f"rcs{b}{h}_{isl}")
                    add("ac", ACTF(recip_r[0:1, 0:P], recip_r[0:1, 0:P], AF.Exp, scale=-1.0),
                        reads=["rcl"], writes=["rcp"], tag=f"rc{b}{h}_{isl}")
                    add("te", MM(ps[4][0:64, 0:P], ones1[0:1, 0:64], recip_r[0:1, 0:P],
                                 True, True),
                        reads=["rcp", "ones1"], writes=["PS4"], tag=f"rp{b}{h}_{isl}")
                    add("ve", CP(o_sb[0:64, 0:P], ps[7][0:64, 0:P]),
                        reads=["PS7"], writes=["osb"], tag=f"oe{b}{h}_{isl}")
                    add("ve", CP(tmp4[5][:], kvec[:]), reads=["kvec"], writes=[],
                        tag=f"osp{b}{h}_{isl}")
                    add("ve", TT(loTp[h // 2].rearrange("p (i c) -> p i c", c=P)
                                 [:, isl, :][(h % 2) * 64:(h % 2) * 64 + 64, :]
                                 if False else
                                 loTp[h // 2][(h % 2) * 64:(h % 2) * 64 + 64,
                                              isl * P:(isl + 1) * P],
                                 o_sb[0:64, 0:P], ps[4][0:64, 0:P], OP.mult),
                        reads=["osb", "PS4"], writes=[f"lo{h}_{isl}"], tag=f"on{b}{h}_{isl}")
            for isl in range(16):
                for eh in range(2):
                    pb = 2 + eh
                    pk = f"PS{pb}"
                    for ic in range(4):
                        add("te", MM(ps[pb][:, 0:512], loTp[ic][:, isl * P:(isl + 1) * P],
                                     woutlb[ic][:, eh * 512:eh * 512 + 512],
                                     ic == 0, ic == 3),
                            reads=[f"lo{ic * 2}_{isl}", f"lo{ic * 2 + 1}_{isl}", f"wL{ic}"],
                            writes=[pk], tag=f"lp{b}{isl}{eh}_{ic}")
                    add("ve", CP(lout[:, eh * 512:eh * 512 + 512], ps[pb][:, 0:512]),
                        reads=[pk], writes=[f"lo_ev{eh}"], tag=f"le{b}{isl}{eh}")
                add("gp", DMA(v.light_dram[b * N + isl * P: b * N + (isl + 1) * P, :],
                              lout[:]),
                    reads=["lo_ev0", "lo_ev1"], writes=[f"ls{b}{isl}"], dma=True,
                    tag=f"lst{b}{isl}")
        sch.barrier()
        if DEBUG_PHASES < 6:
            _finish_debug(v, add, sch, locals())
            sch.schedule(nc, stack)
            _emit_all(sch, nc, stack)
            return nc, v

        wstage = [(di["WqT"][i * P:(i + 1) * P, :], wqb[i], 512, f"wq{i}") for i in range(8)]
        wstage += [(di["WkvT"][i * P:(i + 1) * P, :], wkvb[i], 1024, f"wk{i}") for i in range(8)]
        wstage += [(di["Wout_hT"][i * P:(i + 1) * P, :], wouthb[i], D, f"wh{i}") for i in range(4)]
        for si, (srcw, dst, cols, wk) in enumerate(wstage):
            bi = si % 2
            add("sy", DMA(wstg[bi][:, 0:cols], srcw), writes=[f"wg{bi}"], dma=True,
                tag=f"w6l{si}")
            add("ve", CP(dst[:, 0:cols], wstg[bi][:, 0:cols]),
                reads=[f"wg{bi}"], writes=[wk], tag=f"w6c{si}")
        add("sy", DMA(nullva.rearrange("o (h e) -> o h e", e=65)[:, :, 0:64],
                      di["null_kv"].rearrange("(kv h dh) -> kv h dh", kv=2, h=H)[1, :, :]),
            writes=["nullva"], dma=True, tag="ld_nullv")
        add("ve", MS(nullva.rearrange("o (h e) -> o h e", e=65)[:, :, 64:65], 1.0),
            reads=["nullva"], writes=["nullva"], tag="nullva1")
        add("ve", CP(nullvab[:], nullva[:]), reads=["nullva"], writes=["nullvab"],
            tag="cnullv")
        sch.barrier()

        def rms_rows(tag, gkey, gb, extra_reads=()):
            add("ve", lambda e, gb=gb: e.bn_stats(out=stats[:, 0:6], in_=gat[gb][:, 0:512]),
                reads=[gkey] + list(extra_reads), writes=["gsta"], tag=f"{tag}bn")
            add("ve", lambda e, gb=gb: e.bn_stats(out=stats[:, 6:12], in_=gat[gb][:, 512:1024]),
                reads=[gkey], writes=["gstb"], tag=f"{tag}bn2")
            add("ve", lambda e: e.bn_aggr(out=mv[:, 0:2], in_=stats[:, 0:12]),
                reads=["gsta", "gstb"], writes=["gmv"], tag=f"{tag}ag")
            add("ve", TT(tmp4[0][:, 0:1], mv[:, 0:1], mv[:, 0:1], OP.mult),
                reads=["gmv"], writes=["gm2"], tag=f"{tag}m2")
            add("ve", TT(tmp4[1][:, 0:1], mv[:, 1:2], tmp4[0][:, 0:1], OP.add),
                reads=["gmv", "gm2"], writes=["gss"], tag=f"{tag}ss")
            add("ve", TS(tmp4[1][:, 0:1], tmp4[1][:, 0:1], 1024.0, OP.mult),
                reads=["gss"], writes=["gss2"], tag=f"{tag}sq")
            add("ac", ACTF(st_rs[:, 0:1], tmp4[1][:, 0:1], AF.Ln),
                reads=["gss2"], writes=["gln"], tag=f"{tag}ln")
            add("ac", ACP(xnTf[0][0:1, 0:1], consts[0:1, 2:3]), reads=["consts"],
                writes=[], tag=f"{tag}as")
            add("ac", ACTF(st_rs[:, 0:1], st_rs[:, 0:1], AF.Exp, scale=0.5),
                reads=["gln"], writes=["gnrm"], tag=f"{tag}ex")
            add("ve", TS(st_rs[:, 1:2], st_rs[:, 0:1], consts[:, 1:2], OP.max),
                reads=["gnrm", "consts"], writes=["gmx"], tag=f"{tag}mx")
            add("ve", lambda e: e.reciprocal(out=st_rs[:, 2:3], in_=st_rs[:, 1:2]),
                reads=["gmx"], writes=["ginv"], tag=f"{tag}in")
            add("ve", TS(st_rs[:, 3:4], st_rs[:, 2:3], 32.0, OP.mult),
                reads=["ginv"], writes=["gi32"], tag=f"{tag}i32")
            add("ve", TS(gatb[:], gat[gb][:], st_rs[:, 3:4], OP.mult),
                reads=[gkey, "gi32"], writes=["gatb"], tag=f"{tag}sc")

        for b in range(min(B, P6_NB)):
            for j in range(4):
                gb = j % 2
                add("gp", lambda e, b=b, gb=gb, j=j: e.indirect_dma_start(
                    out=gat[gb][:], out_offset=None, in_=v.xn_dram,
                    in_offset=IndirectOffsetOnAxis(idx_q[b][:, j:j + 1], 0),
                    element_offset=b * N * D, bounds_check=N - 1, oob_is_err=False),
                    reads=[f"ixq{b}"], writes=[f"gq{gb}"], dma=True, tag=f"ga{b}_{j}")
                rms_rows(f"gq{b}{j}", f"gq{gb}", gb)
                for dg in range(8):
                    pb = 4 if dg % 2 == 0 else 6
                    add("te", TRN(ps[pb][:, 0:64].bitcast(BF16)[:, 0:P],
                                  gatb[:, dg * P:(dg + 1) * P], identb[:]),
                        reads=["gatb", "identb"], writes=[f"PS{pb}"], tag=f"gT{b}_{j}{dg}")
                    add("ve", lambda e, j=j, pb=pb, dg=dg: e.tensor_scalar(
                        out=xqT[dg][:, j * P:(j + 1) * P],
                        in0=ps[pb][:, 0:64].bitcast(BF16)[:, 0:P],
                        scalar1=ffng_t[:, dg:dg + 1], scalar2=None, op0=OP.mult),
                        reads=[f"PS{pb}", "ffng_t"], writes=[f"xq{j}_{dg}"],
                        tag=f"gE{b}_{j}{dg}")
            for tt in range(16):
                gb = tt % 2
                add("sy", DMA(gat[gb][:], v.xn_dram[b * N + tt * P: b * N + (tt + 1) * P, :]),
                    reads=[], writes=[f"gq{gb}"], dma=True, tag=f"kl{b}_{tt}")
                rms_rows(f"kv{b}{tt}", f"gq{gb}", gb)
                for dg in range(8):
                    pb = 4 if dg % 2 == 0 else 6
                    add("te", TRN(ps[pb][:, 0:64].bitcast(BF16)[:, 0:P],
                                  gatb[:, dg * P:(dg + 1) * P], identb[:]),
                        reads=["gatb", "identb"], writes=[f"PS{pb}"], tag=f"kT{b}_{tt}{dg}")
                    add("ve", lambda e, tt=tt, pb=pb, dg=dg: e.tensor_scalar(
                        out=xkvT[dg][:, tt * P:(tt + 1) * P],
                        in0=ps[pb][:, 0:64].bitcast(BF16)[:, 0:P],
                        scalar1=ffng_t[:, dg:dg + 1], scalar2=None, op0=OP.mult),
                        reads=[f"PS{pb}", "ffng_t"], writes=[f"xk{tt}_{dg}"],
                        tag=f"kE{b}_{tt}{dg}")
            xkr = [f"xk{tt}_{dg}" for tt in range(16) for dg in range(8)]
            xqr = [f"xq{j}_{dg}" for j in range(4) for dg in range(8)]
            for h in range(min(H, P6_NH)):
                for jt in range(16):
                    for dg in range(8):
                        add("te", MM(ps[6][:, 0:64], xkvT[dg][:, jt * P:(jt + 1) * P],
                                     wkvb[dg][:, h * 128 + 64:h * 128 + 128],
                                     dg == 0, dg == 7),
                            reads=[f"wk{dg}", f"xk{jt}_{dg}"], writes=["PS6"],
                            tag=f"hv{b}{h}{jt}_{dg}")
                    add("ve", lambda e, jt=jt, h=h: e.tensor_copy(
                        out=vaug_h[jt].rearrange("p (hh ee) -> p hh ee", ee=65)[:, h, 0:64],
                        in_=ps[6][:, 0:64]),
                        reads=["PS6"], writes=[f"vh{jt}_{h}"], tag=f"hve{b}{h}{jt}")
            for jt in range(16):
                add("ve", lambda e, jt=jt: e.memset(
                    vaug_h[jt].rearrange("p (hh ee) -> p hh ee", ee=65)[:, :, 64:65], 1.0),
                    reads=[f"vh{jt}_{hh}" for hh in range(H)], writes=[f"vo{jt}"],
                    tag=f"hvo{b}{jt}")
            for h in range(min(H, P6_NH)):
                for dg in range(8):
                    add("te", MM(ps[2][0:64, 0:512], wqb[dg][:, h * 64:h * 64 + 64],
                                 xqT[dg][:], dg == 0, dg == 7),
                        reads=[f"wq{dg}"] + (xqr if dg == 0 else []),
                        writes=["PS2"], tag=f"hq{b}{h}_{dg}")
                add("ve", CP(qT_hb[0:64, 0:512], ps[2][0:64, 0:512]),
                    reads=["PS2"], writes=["qTh"], tag=f"hqe{b}{h}")
                for half in range(4):
                    for dg in range(8):
                        add("te", MM(ps[3][0:64, 0:512], wkvb[dg][:, h * 128:h * 128 + 64],
                                     xkvT[dg][:, half * 512:half * 512 + 512],
                                     dg == 0, dg == 7),
                            reads=[f"wk{dg}"], writes=["PS3"], tag=f"hk{b}{h}{half}_{dg}")
                    add("ve", CP(kT_hb[0:64, half * 512:half * 512 + 512],
                                 ps[3][0:64, 0:512]),
                        reads=["PS3"], writes=[f"kTh{half}"], tag=f"hke{b}{h}{half}")
                add("te", MM(ps[2][0:1, 0:512], nullkTb[:, h:h + 1], qT_hb[0:64, 0:512],
                             True, True),
                    reads=["qTh", "nullkTb"], writes=["PS2"], tag=f"hn{b}{h}")
                add("ac", ACTF(Pn_h[0:1, 0:512], ps[2][0:1, 0:512], AF.Exp, scale=0.125),
                    reads=["PS2"], writes=["Pn"], tag=f"hne{b}{h}")
                add("te", MM(ps[7][0:65, 0:512], nullvab[0:1, h * 65:h * 65 + 65],
                             Pn_h[0:1, 0:512], True, False),
                    reads=["Pn", "nullvab"], writes=["PS7"], tag=f"hon{b}{h}")
                for jc in range(16):
                    add("te", MM(ps[4][:, 0:512], kT_hb[0:64, jc * 128:jc * 128 + 128],
                                 qT_hb[0:64, 0:512], True, True),
                        reads=[f"kTh{jc // 4}", "qTh"], writes=["PS4"],
                        tag=f"hs{b}{h}{jc}")
                    add("ac", ACTF(Pt_h[:, 0:512], ps[4][:, 0:512], AF.Exp, scale=0.125),
                        reads=["PS4"], writes=["Ph0"], tag=f"he{b}{h}{jc}")
                    add("ve", lambda e, b=b, jc=jc: e.tensor_scalar(
                        out=Pt_h[:, 0:512], in0=Pt_h[:, 0:512],
                        scalar1=kvmask[b][:, jc:jc + 1], scalar2=None, op0=OP.mult),
                        reads=["Ph0", f"kvm{b}"], writes=["Ph"], tag=f"hm{b}{h}{jc}",
                        haz_ok=["Ph0"]),
                    add("te", MM(ps[7][0:65, 0:512], vaug_h[jc][:, h * 65:h * 65 + 65],
                                 Pt_h[:, 0:512], False, jc == 15),
                        reads=["Ph", f"vo{jc}"], writes=["PS7"], tag=f"ho{b}{h}{jc}")
                add("ac", ACTF(denr[0:1, 0:512], ps[7][64:65, 0:512], AF.Ln),
                    reads=["PS7"], writes=["rcl"], tag=f"hrl{b}{h}")
                add("ac", ACP(xnTf[0][0:1, 0:1], consts[0:1, 2:3]), reads=["consts"],
                    writes=[], tag=f"hrs{b}{h}")
                add("ac", ACTF(denr[0:1, 0:512], denr[0:1, 0:512], AF.Exp, scale=-1.0),
                    reads=["rcl"], writes=["rcp"], tag=f"hr{b}{h}")
                add("te", MM(ps[5][0:64, 0:512], ones1[0:1, 0:64], denr[0:1, 0:512],
                             True, True),
                    reads=["rcp", "ones1"], writes=["PS5"], tag=f"hrep{b}{h}")
                add("ve", CP(o_sbh[0:64, 0:512], ps[7][0:64, 0:512]),
                    reads=["PS7"], writes=["osbh"], tag=f"hoe{b}{h}")
                add("ve", TT(hoT[h // 2][(h % 2) * 64:(h % 2) * 64 + 64, 0:512],
                             o_sbh[0:64, 0:512], ps[5][0:64, 0:512], OP.mult),
                    reads=["osbh", "PS5"], writes=[f"hoT{h}"], tag=f"hno{b}{h}")
            for itile in range(4 if P6_NH >= 8 else 0):
                for eh in range(2):
                    pb = 2 + eh
                    pk = f"PS{pb}"
                    for ic in range(4):
                        add("te", MM(ps[pb][:, 0:512],
                                     hoT[ic][:, itile * P:(itile + 1) * P],
                                     wouthb[ic][:, eh * 512:eh * 512 + 512],
                                     ic == 0, ic == 3),
                            reads=[f"hoT{ic * 2}", f"hoT{ic * 2 + 1}", f"wh{ic}"],
                            writes=[pk], tag=f"hp{b}{itile}{eh}_{ic}")
                    add("ve", CP(hvout[:, eh * 512:eh * 512 + 512], ps[pb][:, 0:512]),
                        reads=[pk], writes=[f"hv_ev{eh}"], tag=f"hpe{b}{itile}{eh}")
                add("sy", DMA(v.heavy_dram[b * 520 + itile * P:
                                           b * 520 + (itile + 1) * P, :], hvout[:]),
                    reads=["hv_ev0", "hv_ev1"],
                    writes=[f"hsc{b}{itile}"], dma=True, tag=f"hsw{b}{itile}")
        sch.barrier()

        if DEBUG_PHASES < 7:
            _finish_debug(v, add, sch, locals())
            sch.schedule(nc, stack)
            _emit_all(sch, nc, stack)
            return nc, v
        for i in range(NTILE):
            bi = i % 2
            sl = slice(i * P, (i + 1) * P)
            add("sy", DMA(f_x[bi][:], di["x"][sl, :]), writes=[f"fx{bi}"], dma=True,
                tag=f"flx{i}")
            add("sy", DMA(f_xn[bi][:], v.xn_dram[sl, :]), writes=[f"fxn{bi}"], dma=True,
                tag=f"flxn{i}")
            add("sy", DMA(f_sp[bi][:], v.sp_dram[sl, :]), writes=[f"fsp{bi}"], dma=True,
                tag=f"flsp{i}")
            add("sy", DMA(f_li[bi][:], v.light_dram[sl, :]), writes=[f"fli{bi}"], dma=True,
                tag=f"flli{i}")
            add("gp", lambda e, i=i, bi=bi: e.indirect_dma_start(
                out=f_ff[bi][:], out_offset=None, in_=v.heavy_dram,
                in_offset=IndirectOffsetOnAxis(qidx_t[:, i:i + 1], 0),
                element_offset=(i // 16) * 520 * D, bounds_check=519,
                oob_is_err=False),
                reads=["qidx"], writes=[f"fff{bi}"], dma=True, tag=f"flff{i}")
            add("ve", TT(f_t[0][:], f_ff[bi][:], f_li[bi][:], OP.add),
                reads=[f"fff{bi}", f"fli{bi}"], writes=["ft0"], tag=f"fa{i}")
            add("ve", TT(f_t[1][:], f_xn[bi][:], f_sp[bi][:], OP.add),
                reads=[f"fxn{bi}", f"fsp{bi}"], writes=["ft1"], tag=f"fb{i}")
            add("ac", ACTF(f_t[2][:], f_t[0][:], AF.Silu),
                reads=["ft0"], writes=["ft2"], tag=f"fs{i}")
            add("ve", TT(f_t[3][:], f_t[1][:], f_t[2][:], OP.mult),
                reads=["ft1", "ft2"], writes=["ft3"], tag=f"fm{i}")
            add("ve", TT(f_o[bi][:], f_x[bi][:], f_t[3][:], OP.add),
                reads=[f"fx{bi}", "ft3"], writes=[f"fo{bi}"], tag=f"fo{i}")
            add("gp", DMA(v.out_d[sl, :], f_o[bi][:]),
                reads=[f"fo{bi}"], writes=[f"os{i}"], dma=True, tag=f"fst{i}")
        add("gp", lambda e: None, reads=[f"os{NTILE - 1}", f"os{NTILE - 2}",
                                         f"os{NTILE - 3}"], tag="fin")

        _finish_debug(v, add, sch, locals())
        sch.schedule(nc, stack)
        _emit_all(sch, nc, stack)
    return nc, v


def _finish_debug(v, add, sch, env):
    for name, shape, fn in TAPS:
        ap, reads = fn(env)
        add("gp", DMA(v.dbg[name], ap), reads=reads, writes=[f"_dbg{name}"], dma=True,
            tag=f"tap{name}")
        add("gp", lambda e: None, reads=[f"_dbg{name}"], tag=f"tapw{name}")


def _emit_all(sch, nc, stack):
    block = stack.enter_context(nc.Block())

    @block.sync
    def _(sy):
        sch.emit("sy", sy)

    @block.gpsimd
    def _(gp):
        sch.emit("gp", gp)

    @block.vector
    def _(ve):
        sch.emit("ve", ve)

    @block.scalar
    def _(ac):
        sch.emit("ac", ac)

    @block.tensor
    def _(te):
        sch.emit("te", te)


def _host_inputs(core, inp):
    import ml_dtypes
    b0 = 2 * core
    d = {}
    d["x"] = np.ascontiguousarray(
        np.asarray(inp["L0_attn_x"][b0:b0 + 2], dtype=np.float32).reshape(TOK, D))
    d["W_in"] = np.ascontiguousarray(np.asarray(inp["L0_attn_W_in"], dtype=np.float32))
    d["W_out"] = np.ascontiguousarray(np.asarray(inp["L0_attn_W_out"], dtype=np.float32))
    d["WqkvT"] = np.ascontiguousarray(np.asarray(inp["L0_ffn_Wqkv"], dtype=np.float32).T)
    d["Wout_lT"] = np.ascontiguousarray(np.asarray(inp["L0_ffn_Wout_l"], dtype=np.float32).T)
    d["WqT"] = np.ascontiguousarray(np.asarray(inp["L0_ffn_Wq"], dtype=np.float32).T)
    d["WkvT"] = np.ascontiguousarray(np.asarray(inp["L0_ffn_Wkv"], dtype=np.float32).T)
    d["Wout_hT"] = np.ascontiguousarray(np.asarray(inp["L0_ffn_Wout_h"], dtype=np.float32).T)

    def tposed(vec):
        return np.ascontiguousarray(np.asarray(vec, dtype=np.float32).reshape(8, P).T)

    d["A_t"] = tposed(inp["L0_attn_A"])
    d["gamma_t"] = tposed(inp["L0_attn_gamma"])
    d["beta_t"] = tposed(inp["L0_attn_beta"])
    d["lnw_t"] = tposed(inp["L0_ffn_ln_w"])
    d["lnb_t"] = tposed(inp["L0_ffn_ln_b"])
    d["ffng_t"] = tposed(inp["L0_ffn_gamma"])
    d["b_in_t"] = np.ascontiguousarray(
        np.asarray(inp["L0_attn_b_in"], dtype=np.float32).reshape(16, P).T)
    d["b_out_row"] = np.asarray(inp["L0_attn_b_out"], dtype=np.float32).reshape(1, D)
    d["null_q_row"] = np.asarray(inp["L0_ffn_null_q"], dtype=np.float32).reshape(1, D)
    d["null_kv"] = np.ascontiguousarray(
        np.asarray(inp["L0_ffn_null_kv"], dtype=np.float32).reshape(-1))
    d["rt2"] = np.ascontiguousarray(np.stack(
        [np.asarray(inp["L0_ffn_rt_q"], dtype=np.float32).reshape(D),
         np.asarray(inp["L0_ffn_rt_kv"], dtype=np.float32).reshape(D)], axis=1))
    d["ones1"] = np.ones((1, 512), np.float32)
    d["ident"] = np.eye(P, dtype=np.float32)
    d["identb"] = np.eye(P, dtype=np.float32).astype(ml_dtypes.bfloat16)
    d["ltri"] = np.ascontiguousarray(np.triu(np.ones((P, P), np.float32), 1))
    d["onesP"] = np.ones((P, P), np.float32)
    ci = 2.0 - (np.arange(P)[:, None] * 16 + np.arange(16)[None, :]) * EPSK
    d["cidx"] = np.ascontiguousarray(np.tile(ci, (1, 4)).astype(np.float32))
    d["kvec"] = np.ascontiguousarray(
        np.tile(np.array([NQ, NQ, NKV, NKV], np.float32), (P, 1)))
    lk = np.log(np.array([576.0, 576.0, 1152.0, 1152.0]))
    d["logk"] = np.ascontiguousarray(np.tile(lk, (P, 1)).astype(np.float32))
    d["na0"] = np.ascontiguousarray(
        np.tile(np.log(2048.0) - lk, (P, 1)).astype(np.float32))
    mL = np.zeros((P, P), np.float32)
    mL[64:, 0:64] = 1.0
    d["maskLt"] = mL.astype(ml_dtypes.bfloat16)
    mR = np.zeros((P, P), np.float32)
    mR[0:64, 64:] = 1.0
    d["maskRt"] = mR.astype(ml_dtypes.bfloat16)
    d["IOTA"] = np.ascontiguousarray(np.tile(np.arange(P, dtype=np.float32)[None, :], (P, 1)))
    d["pcol"] = np.ascontiguousarray(np.arange(P, dtype=np.float32)[:, None])
    cc = np.zeros((P, 8), np.float32)
    cc[:, 0] = LNEPS
    cc[:, 1] = 1e-12
    cc[:, 2] = 1.0
    d["consts"] = cc
    return d


def kernel(**inputs):
    nc, v = build_program()
    in_maps = [_host_inputs(core, inputs) for core in range(8)]
    res = run_bass_kernel_spmd(nc, in_maps, core_ids=list(range(8)))
    outs = [np.asarray(r["outp"], dtype=np.float32).reshape(2, N, D)
            for r in res.results]
    return np.concatenate(outs, axis=0)
